# version 02 (final)
# baseline (speedup 1.0000x reference)
import contextlib
import numpy as np
import ml_dtypes
import concourse.bass as bass
import concourse.mybir as mybir
from concourse.bass_utils import run_bass_kernel_spmd

F32 = mybir.dt.float32
BF16 = mybir.dt.bfloat16
I32 = mybir.dt.int32
AF = mybir.ActivationFunctionType
OP = mybir.AluOpType
AX = mybir.AxisListType

T = 4096
D = 1024
DI = 2048
NI = 2816
NST = 16
KD = 8
KDI = 16
KNI = 22
NCH = 8
SCH = 512
LN_EPS = 1e-5

ENGS = ("sync", "tensor", "vector", "scalar", "gpsimd")
ESEM = {"sync": "sq", "tensor": "stt", "vector": "sv", "scalar": "ssc",
        "gpsimd": "sgd"}
EINC = {"sync": 16, "tensor": 1, "vector": 1, "scalar": 1, "gpsimd": 16}
NOPSEM = {"sync": "sqn", "tensor": "stt", "vector": "sv", "scalar": "ssc",
          "gpsimd": "sgn"}


def bcast_dram(ap_nd, p=128):
    return bass.AP(tensor=ap_nd.tensor, offset=ap_nd.offset,
                   ap=[[0, p]] + [list(x) for x in ap_nd.ap])


class Sched:
    def __init__(self):
        self.q = {e: [] for e in ENGS}
        self.cnt = {"sq": 0, "stt": 0, "sv": 0, "ssc": 0, "sgd": 0,
                    "sod": 0, "sqn": 0, "sgn": 0, "sgm": 0}
        self.markers = []

    def emit(self, eng, fn):
        self.q[eng].append(fn)

    def mark(self, eng, fn, sem=None, k=None):
        sem = sem or ESEM[eng]
        k = k if k is not None else EINC[eng]
        val = self.cnt[sem] + k
        self.cnt[sem] = val

        def g(e, S, fn=fn, sem=sem, k=k):
            op = fn(e, S)
            op.then_inc(S[sem], k)
        self.q[eng].append(g)
        return val

    def wait(self, eng, sem, val):
        if val <= 0:
            return
        if sem == "sgd":
            m = None
            for cover, sv_ in self.markers:
                if cover >= val:
                    m = sv_
                    break
            if m is None:
                self.cnt["sgm"] += 16
                m = self.cnt["sgm"]
                self.markers.append((self.cnt["sgd"], m))

                def mk(e, S):
                    e.dma_start(out=S["_dmark"],
                                in_=S["_msrc"]).then_inc(S["sgm"], 16)
                self.q["gpsimd"].append(mk)
            self.q[eng].append(lambda e, S, v=m: e.wait_ge(S["sgm"], v))
            return
        self.q[eng].append(lambda e, S, s=sem, v=val: e.wait_ge(S[s], v))

    def wait_counts(self, eng, sems):
        for s in sems:
            self.wait(eng, s, self.cnt[s])

    def barrier(self, engines=ENGS, tag=""):
        for eng in engines:
            self.mark(eng, lambda e, S: e.nop(), NOPSEM[eng], 1)
        snap = dict(self.cnt)
        for eng in engines:
            for s, v in snap.items():
                if v > 0:
                    self.wait(eng, s, v)


def build_program(phases=5, sub=7, dbg=0):
    nc = bass.Bass("TRN2", target_bir_lowering=False,
                   detect_race_conditions=False)

    def inp(name, shape, dt=F32):
        return nc.dram_tensor(name, shape, dt, kind="ExternalInput").ap()

    x0 = inp("x0", [T, D])
    ln_g = inp("ln_g", [1, D])
    ln_b = inp("ln_b", [1, D])
    w_inT = inp("w_inT", [D, 2 * DI])
    conv_w = inp("conv_w", [DI, 4])
    conv_b = inp("conv_b", [1, DI])
    w_xT = inp("w_xT", [DI, 33])
    w_dt = inp("w_dt", [1, DI])
    b_dt = inp("b_dt", [1, DI])
    A_log = inp("A_log", [DI, NST])
    D_p = inp("D_p", [1, DI])
    w_outT = inp("w_outT", [DI, D])
    wg = inp("wg", [NI, D])
    wu = inp("wu", [NI, D])
    wd = inp("wd", [D, NI])
    id16 = inp("id16", [128, 128], BF16)
    id32 = inp("id32", [128, 128], F32)

    out = nc.dram_tensor("out", [T, D], F32, kind="ExternalOutput").ap()

    def scr(name, shape, dt):
        return nc.dram_tensor(name, shape, dt).ap()

    d_wqgu = scr("d_wqgu", [KD, 128, 2 * NI], BF16)
    d_wqd = scr("d_wqd", [KNI, 128, D], BF16)
    d_winT = scr("d_winT", [KD, 128, 2 * DI], BF16)
    d_woutT = scr("d_woutT", [KDI, 128, D], BF16)
    d_xb = scr("d_xb", [KDI, 128, T], BF16)
    d_z = scr("d_z", [KDI, 128, T], BF16)
    d_xcA = scr("d_xcA", [KDI // 2, 128, T], BF16)
    d_xcB = scr("d_xcB", [KDI // 2, 128, T], BF16)

    def d_xc_t(dt):
        return d_xcA[dt] if dt < 8 else d_xcB[dt - 8]
    d_x1y = scr("d_x1y", [T, D], F32)
    d_dtr = scr("d_dtr", [1, T], F32)
    d_bc = scr("d_bc", [32, T], BF16)
    d_yp = scr("d_yp", [KDI, 128, T], BF16)
    d_dbgb = scr("d_dbgb", [2, 128, 2048], BF16)

    stck = contextlib.ExitStack()

    def sbt(name, shape, dt=F32):
        return stck.enter_context(nc.sbuf_tensor(name, shape, dt))

    PF = sbt("PF", [128, 19456], F32)
    PB = sbt("PB", [128, 34816], BF16)
    GMBC = sbt("GMBC", [128, 1024], BF16)
    BEBC = sbt("BEBC", [128, 1024], BF16)
    qU2 = sbt("qU2", [128, 704], F32)
    qI = sbt("qI", [128, 704], I32)
    qA2 = sbt("qA2", [128, 704], F32)
    qB = sbt("qB", [128, 6, 704], BF16)
    qSG = sbt("qSG", [128, 704], BF16)
    qBM = sbt("qBM", [128, 44], F32)
    qHS = sbt("qHS", [128, 44], F32)
    qR2 = sbt("qR2", [128, 44], F32)
    qHSB = sbt("qHSB", [128, 44], BF16)
    qEPS = sbt("qEPS", [128, 2], F32)
    A_sb = sbt("A_sb", [128, 16, 16], F32)
    WDT = sbt("WDT", [128, 16], F32)
    BDT = sbt("BDT", [128, 16], F32)
    CW = sbt("CW", [128, 16, 4], F32)
    CB = sbt("CB", [128, 16], F32)
    DP = sbt("DP", [128, 16], F32)
    GM = sbt("GM", [128, 8], F32)
    BE = sbt("BE", [128, 8], F32)
    BEB = sbt("BEB", [128, 8], BF16)
    WXT = sbt("WXT", [128, 16, 33], BF16)
    IDB = sbt("IDB", [128, 128], BF16)
    IDF = sbt("IDF", [128, 128], F32)
    ST6 = sbt("ST6", [128, 6], F32)
    ST2 = sbt("ST2", [128, 4], F32)
    MU = sbt("MU", [128, 4], F32)
    RSD = sbt("RSD", [128, 4], F32)
    SMA = sbt("SMA", [128, 4], F32)
    SMB = sbt("SMB", [128, 4], F32)
    HLAST = sbt("HLAST", [128, 256], F32)
    WS = sbt("WS", [128, 24, 128], BF16)
    SGT = sbt("SGT", [128, 512], BF16)
    HM = sbt("HM", [128, 512], BF16)
    ZB = sbt("ZB", [128, 512], BF16)
    SSME = sbt("SSME", [128, 512], F32)
    SSMB = sbt("SSMB", [128, 512], BF16)
    DTTB = sbt("DTTB", [128, SCH], BF16)
    DTXB = sbt("DTXB", [128, SCH], BF16)
    ABF = sbt("ABF", [128, SCH], BF16)
    UBF = sbt("UBF", [128, SCH], BF16)
    YPBT = sbt("YPBT", [128, SCH], BF16)
    XCS = sbt("XCS", [128, SCH], BF16)

    oX0, oX1, oYT, oSC = 0, 4096, 8192, 12288
    oW32 = 16384
    oCVA, oCVB = 0, 2048
    oDTR, oPRE, oSPA, oYD = 8192, 8704, 9216, 9728
    bXQ, bXN2, bRQ, bHTOK, bHQ = 0, 4096, 8192, 12288, 23552
    bRHQ = bHTOK
    bFDB = bXQ
    bXBT, bXCT = 0, 4096
    sBBC, sCBC, sHT, sPT = 0, 8192, 16384, 24576
    oTRA = 10240
    tZT, tYP, tGG = 0, 8192, 16384
    tMOB = tYP
    t2HT, t2HQ, t2RHQ = 0, 11264, 22528
    t2XQ = 16384
    t2RQ = 33792
    t2RQ = 20480
    t2FDB = tYP
    t2XN = 24576

    EVT = PB[:, 12288:12288 + 4096]

    psA = stck.enter_context(nc.psum_tensor("psA", [128, 512], F32))
    psB = stck.enter_context(nc.psum_tensor("psB", [128, 512], F32))
    psC = stck.enter_context(nc.psum_tensor("psC", [128, 512], F32))
    psD = stck.enter_context(nc.psum_tensor("psD", [128, 512], F32))
    psS = stck.enter_context(nc.psum_tensor("psS", [128, 512], F32))
    psU = stck.enter_context(nc.psum_tensor("psU", [128, 512], F32))
    psT1 = stck.enter_context(nc.psum_tensor("psT1", [128, 1024], BF16))
    psT2 = stck.enter_context(nc.psum_tensor("psT2", [128, 1024], BF16))

    S = {}
    for s in ["sq", "stt", "sv", "ssc", "sgd", "sod", "sqn", "sgn", "sgm"]:
        S[s] = stck.enter_context(nc.semaphore(s))
    d_mark = scr("d_mark", [1, 2], F32)
    S["_dmark"] = d_mark
    S["_msrc"] = qEPS[0:1, 0:2]

    sch = Sched()
    SUBG = [sub]
    ws_last = [0]

    def ld(dst_ap, src_ap):
        return sch.mark("sync", lambda e, _S, d=dst_ap, s=src_ap:
                        e.dma_start(out=d, in_=s))

    def stg(dst_ap, src_ap):
        return sch.mark("gpsimd", lambda e, _S, d=dst_ap, s=src_ap:
                        e.dma_start(out=d, in_=s))

    def vop(fn):
        sch.emit("vector", fn)

    def vmark(fn):
        return sch.mark("vector", fn)

    def sop(fn):
        sch.emit("scalar", fn)

    def smark(fn):
        return sch.mark("scalar", fn)

    def tmark(fn):
        return sch.mark("tensor", fn)

    def top(fn):
        sch.emit("tensor", fn)

    def quant_piece(in_ap, out_ap, W):
        nb = W // 16
        in3 = in_ap.rearrange("p (a b) -> p a b", b=16)

        def v1(e, _S):
            return e.tensor_reduce(out=qBM[:, 0:nb], in_=in3, axis=AX.X,
                                   op=OP.max, apply_absolute_value=True)
        vu0 = vmark(v1)

        def s_hs(e, _S):
            e.wait_ge(S["sv"], vu0)
            e.activation(out=qHS[:, 0:nb], in_=qBM[:, 0:nb], func=AF.Ln,
                         scale=1.0 / 12.0, bias=qEPS[:, 0:1])
            e.activation(out=qR2[:, 0:nb], in_=qHS[:, 0:nb], func=AF.Exp,
                         scale=-1.0)
            return e.activation(out=qHSB[:, 0:nb], in_=qHS[:, 0:nb],
                                func=AF.Exp)
        shs = smark(s_hs)

        def v1b(e, _S):
            e.wait_ge(S["ssc"], shs)
            return e.tensor_tensor(
                out=qU2[:, 0:W].rearrange("p (a b) -> p a b", b=16),
                in0=in3,
                in1=qR2[:, 0:nb, None].broadcast_to([128, nb, 16]),
                op=OP.mult)
        vu = vmark(v1b)

        def s1(e, _S):
            e.wait_ge(S["sv"], vu)
            e.activation(out=qA2[:, 0:W], in_=qU2[:, 0:W], func=AF.Abs)
            return e.sign(out=qSG[:, 0:W], in_=qU2[:, 0:W])
        su = smark(s1)

        def v2(e, _S):
            e.wait_ge(S["ssc"], su)
            e.tensor_copy(qI[:, 0:W], qU2[:, 0:W])
            e.tensor_copy(qB[:, 0, 0:W], qI[:, 0:W])
            e.tensor_scalar(out=qB[:, 1, 0:W], in0=qA2[:, 0:W], scalar1=5.0,
                            scalar2=2.0, op0=OP.is_gt, op1=OP.mult)
            e.tensor_scalar(out=qB[:, 2, 0:W], in0=qA2[:, 0:W], scalar1=7.0,
                            scalar2=2.0, op0=OP.is_gt, op1=OP.mult)
            e.tensor_scalar(out=qB[:, 3, 0:W], in0=qA2[:, 0:W], scalar1=10.0,
                            scalar2=4.0, op0=OP.is_gt, op1=OP.mult)
            e.tensor_tensor(out=qB[:, 4, 0:W], in0=qB[:, 1, 0:W],
                            in1=qB[:, 2, 0:W], op=OP.add)
            e.tensor_tensor(out=qB[:, 5, 0:W], in0=qB[:, 4, 0:W],
                            in1=qB[:, 3, 0:W], op=OP.add)
            e.tensor_tensor(out=qB[:, 4, 0:W], in0=qB[:, 5, 0:W],
                            in1=qSG[:, 0:W], op=OP.mult)
            e.tensor_tensor(out=qA2[:, 0:W], in0=qU2[:, 0:W],
                            in1=qB[:, 0, 0:W], op=OP.subtract)
            e.tensor_scalar(out=qB[:, 1, 0:W], in0=qA2[:, 0:W], scalar1=0.5,
                            scalar2=None, op0=OP.is_gt)
            e.tensor_scalar(out=qB[:, 2, 0:W], in0=qA2[:, 0:W], scalar1=-0.5,
                            scalar2=None, op0=OP.is_lt)
            e.tensor_tensor(out=qB[:, 3, 0:W], in0=qB[:, 1, 0:W],
                            in1=qB[:, 2, 0:W], op=OP.subtract)
            e.tensor_tensor(out=qB[:, 1, 0:W], in0=qB[:, 0, 0:W],
                            in1=qB[:, 3, 0:W], op=OP.add)
            e.tensor_scalar(out=qB[:, 2, 0:W], in0=qB[:, 1, 0:W],
                            scalar1=-4.0, scalar2=4.0, op0=OP.max,
                            op1=OP.min)
            e.tensor_tensor(out=qB[:, 0, 0:W], in0=qB[:, 4, 0:W],
                            in1=qB[:, 2, 0:W], op=OP.add)
            return e.tensor_tensor(
                out=out_ap.rearrange("p (a b) -> p a b", b=16),
                in0=qB[:, 0, 0:W].rearrange("p (a b) -> p a b", b=16),
                in1=qHSB[:, 0:nb, None].broadcast_to([128, nb, 16]),
                op=OP.mult)
        return vmark(v2)

    def quant_tile(in_ap, out_ap, F):
        last = 0
        pos = 0
        while pos < F:
            W = min(704, F - pos)
            if F - pos - W != 0 and (F - pos - W) % 16 != 0:
                W = F - pos
            last = quant_piece(in_ap[:, pos:pos + W],
                               out_ap[:, pos:pos + W], W)
            pos += W
        return last

    def ln_stats(x_ap, j):
        scrq = PF[:, oW32 + 1024 * (j % 2):oW32 + 1024 * (j % 2) + 1024]

        def v1(e, _S):
            return e.tensor_reduce(out=SMA[:, j:j + 1], in_=x_ap, axis=AX.X,
                                   op=OP.add)
        vu = vmark(v1)

        def s0(e, _S):
            e.wait_ge(S["sv"], vu)
            e.activation(out=scrq, in_=x_ap, func=AF.Square)
            e.activation(out=MU[:, j:j + 1], in_=SMA[:, j:j + 1],
                         func=AF.Copy, scale=1.0 / 1024)
            return e.activation(out=ST6[:, j:j + 1], in_=MU[:, j:j + 1],
                                func=AF.Square)
        s_sq = smark(s0)

        def v2(e, _S):
            e.wait_ge(S["ssc"], s_sq)
            e.tensor_reduce(out=SMB[:, j:j + 1], in_=scrq, axis=AX.X,
                            op=OP.add)
            return e.tensor_scalar(out=ST2[:, j:j + 1], in0=ST6[:, j:j + 1],
                                   scalar1=-1.0, scalar2=LN_EPS,
                                   op0=OP.mult, op1=OP.add)
        vu2 = vmark(v2)

        def s1(e, _S):
            e.wait_ge(S["sv"], vu2)
            e.activation(out=ST6[:, j:j + 1], in_=SMB[:, j:j + 1],
                         func=AF.Ln, scale=1.0 / 1024, bias=ST2[:, j:j + 1])
            return e.activation(out=RSD[:, j:j + 1], in_=ST6[:, j:j + 1],
                                func=AF.Exp, scale=-0.5)
        su = smark(s1)
        return vu2, su

    def ln_apply(x_ap, out_ap, j, su):
        def v1(e, _S):
            e.wait_ge(S["ssc"], su)
            return e.tensor_scalar(out=out_ap, in0=x_ap,
                                   scalar1=MU[:, j:j + 1],
                                   scalar2=RSD[:, j:j + 1],
                                   op0=OP.subtract, op1=OP.mult)
        return vmark(v1)

    def transpose_batch(pieces, pst, ident, wait_sv=0, wait_ssc=0):
        def t1(e, _S):
            if wait_sv:
                e.wait_ge(S["sv"], wait_sv)
            if wait_ssc:
                e.wait_ge(S["ssc"], wait_ssc)
            op = None
            for in_ap, col in pieces:
                op = e.transpose(pst[:, col:col + 128], in_ap, ident[:, :])
            return op
        return tmark(t1)

    ln_g_t = ln_g.rearrange("o (k p) -> o k p", p=128)
    ln_b_t = ln_b.rearrange("o (k p) -> o k p", p=128)
    conv_b_t = conv_b.rearrange("o (k p) -> o k p", p=128)
    w_dt_t = w_dt.rearrange("o (k p) -> o k p", p=128)
    b_dt_t = b_dt.rearrange("o (k p) -> o k p", p=128)
    D_p_t = D_p.rearrange("o (k p) -> o k p", p=128)

    _skip_p0 = bool(sub & 4194304)
    for k in range(0 if _skip_p0 else 8):
        ld(GM[:, k:k + 1], ln_g_t[0, k, :, None])
        ld(BE[:, k:k + 1], ln_b_t[0, k, :, None])
    for k in range(0 if _skip_p0 else 16):
        ld(CB[:, k:k + 1], conv_b_t[0, k, :, None])
        ld(WDT[:, k:k + 1], w_dt_t[0, k, :, None])
        ld(BDT[:, k:k + 1], b_dt_t[0, k, :, None])
        ld(DP[:, k:k + 1], D_p_t[0, k, :, None])
        ld(CW[:, k, :], conv_w.rearrange("(k p) c -> k p c", p=128)[k])
    ld(IDB[:, :], id16)
    ld(IDF[:, :], id32)
    vmark(lambda e, _S: e.memset(qEPS[:, :], 5e-13))
    for k in range(0 if _skip_p0 else 16):
        ld(PF[:, oW32 + 16 * k:oW32 + 16 * (k + 1)],
           A_log.rearrange("(k p) n -> k p n", p=128)[k])
        ld(PF[:, oW32 + 256 + 33 * k:oW32 + 256 + 33 * (k + 1)],
           w_xT.rearrange("(k p) n -> k p n", p=128)[k])
    if phases >= 0:
        sch.barrier(tag="params")

    if _skip_p0:
        pass
    def s_a(e, _S):
        e.activation(out=PF[:, oW32 + 1024:oW32 + 1280],
                     in_=PF[:, oW32:oW32 + 256], func=AF.Exp)
        e.activation(out=A_sb[:, :, :].rearrange("p a b -> p (a b)"),
                     in_=PF[:, oW32 + 1024:oW32 + 1280], func=AF.Copy,
                     scale=-1.0)
        return e.activation(out=WXT[:, :, :].rearrange("p a b -> p (a b)"),
                            in_=PF[:, oW32 + 256:oW32 + 256 + 528],
                            func=AF.Copy)
    if not _skip_p0:
        sa1 = smark(s_a)
        vop(lambda e, _S: e.memset(HLAST[:, :], 0.0))
        smark(lambda e, _S: e.activation(out=BEB[:, :], in_=BE[:, :],
                                         func=AF.Copy))
    if phases >= 0:
        sch.barrier(tag="p0b")

    def wprep(src, n_tiles, Wcols, dst, dst_cols_off):
        nk = Wcols // 128
        src_t = src.rearrange("(m p) k -> m p k", p=128)
        for m in range(n_tiles):
            lv = ld(PF[:, oW32:oW32 + Wcols], src_t[m])
            sch.wait("vector", "sq", lv)
            qv = quant_tile(PF[:, oW32:oW32 + Wcols], EVT[:, 0:Wcols], Wcols)
            for b0 in range(0, nk, 8):
                nb8 = min(8, nk - b0)
                pieces = [(EVT[:, 128 * (b0 + jj):128 * (b0 + jj + 1)],
                           128 * jj) for jj in range(nb8)]
                tv = transpose_batch(pieces, psT1, IDB, wait_sv=qv)
                sch.wait("vector", "stt", tv)
                sch.wait("vector", "sgd", sch.cnt["sgd"])
                ev = vmark(lambda e, _S, nb8=nb8:
                           e.tensor_copy(PB[:, bXQ:bXQ + 128 * nb8],
                                         psT1[:, 0:128 * nb8]))
                sch.wait("tensor", "sv", ev)
                sch.wait("gpsimd", "sv", ev)
                for jj in range(nb8):
                    stg(dst[b0 + jj][:, dst_cols_off + 128 * m:
                                     dst_cols_off + 128 * (m + 1)],
                        PB[:, bXQ + 128 * jj:bXQ + 128 * (jj + 1)])
            sch.wait("sync", "sv", sch.cnt["sv"])
            sch.wait("sync", "sgd", sch.cnt["sgd"])

    if phases >= 1 and (sub & 1):
        wprep(wg, KNI, D, d_wqgu, 0)
        wprep(wu, KNI, D, d_wqgu, NI)
        wprep(wd, KD, NI, d_wqd, 0)

    w_inT_t = w_inT.rearrange("(k p) f -> k p f", p=128)
    for k in range(8 if (phases >= 1 and (sub & 2)) else 0):
        lv = ld(PF[:, oX0:oX0 + 4096], w_inT_t[k])
        sch.wait("scalar", "sq", lv)
        sv_c = smark(lambda e, _S: e.activation(
            out=PB[:, bRQ:bRQ + 4096], in_=PF[:, oX0:oX0 + 4096],
            func=AF.Copy))
        sch.wait("gpsimd", "ssc", sv_c)
        stg(d_winT[k], PB[:, bRQ:bRQ + 4096])
        sch.wait("sync", "ssc", sch.cnt["ssc"])
        sch.wait("sync", "sgd", sch.cnt["sgd"])
    if phases >= 1 and not (sub & 2097152):
        lv = ld(PF[:, oX1:oX1 + 1024].rearrange("p (o c) -> p o c", o=1),
                bcast_dram(ln_g))
        lv = ld(PF[:, oX1 + 1024:oX1 + 2048].rearrange("p (o c) -> p o c",
                                                       o=1),
                bcast_dram(ln_b))
        sch.wait("scalar", "sq", lv)
        smark(lambda e, _S: e.activation(out=GMBC[:, :],
                                         in_=PF[:, oX1:oX1 + 1024],
                                         func=AF.Copy))
        smark(lambda e, _S: e.activation(out=BEBC[:, :],
                                         in_=PF[:, oX1 + 1024:oX1 + 2048],
                                         func=AF.Copy))

    w_outT_t = w_outT.rearrange("(k p) f -> k p f", p=128)
    for k in range(16 if (phases >= 1 and (sub & 4)) else 0):
        lv = ld(PF[:, oX0:oX0 + 1024], w_outT_t[k])
        sch.wait("scalar", "sq", lv)
        sv_c = smark(lambda e, _S: e.activation(
            out=EVT[:, 0:1024], in_=PF[:, oX0:oX0 + 1024], func=AF.Copy))
        sch.wait("gpsimd", "ssc", sv_c)
        stg(d_woutT[k], EVT[:, 0:1024])
        sch.wait("sync", "ssc", sch.cnt["ssc"])
        sch.wait("sync", "sgd", sch.cnt["sgd"])
    if phases >= 0:
        sch.barrier(tag="ph1")

    def emit_ffn(c, x_in_off, pbXQ, pbRQ, pbHT, pbHQ, pbRHQ, pbFDB, wsel,
                 out_fn, parts=7):
        for j in range(0 if (wsel < 0) else 4):
            if SUBG[0] & 67108864:
                qv = sch.cnt["sv"]
            else:
                qv = quant_tile(
                    PF[:, x_in_off + 1024 * j:x_in_off + 1024 * (j + 1)],
                    PB[:, pbXQ + 1024 * j:pbXQ + 1024 * (j + 1)], D)
            if SUBG[0] & 134217728:
                continue
            pieces = [(PB[:, pbXQ + 1024 * j + 128 * k:
                          pbXQ + 1024 * j + 128 * (k + 1)], 128 * k)
                      for k in range(8)]
            tv = transpose_batch(pieces, psT2, IDB, wait_sv=qv)
            sch.wait("vector", "stt", tv)
            ev = vmark(lambda e, _S, j=j: e.tensor_copy(
                PB[:, pbRQ:pbRQ + 4096].rearrange(
                    "p (k c) -> p k c", c=512)[:, :, 128 * j:128 * (j + 1)],
                psT2[:, :].rearrange("p (k c) -> p k c", c=128)))
            sch.wait("tensor", "sv", ev)
        rq3 = PB[:, pbRQ:pbRQ + 4096].rearrange("p (k c) -> p k c", c=512)
        for m in range(KNI if parts & 2 else 0):
            wsb = WS
            sch.wait("sync", "stt", ws_last[0])
            lvs = 0
            for k in range(8):
                lvs = ld(wsb[:, k, :], d_wqgu[k][:, 128 * m:128 * (m + 1)])
                lvs = ld(wsb[:, 8 + k, :],
                         d_wqgu[k][:, NI + 128 * m:NI + 128 * (m + 1)])
            psG = psA if m % 2 == 0 else psC
            psUp = psB if m % 2 == 0 else psU

            def t_m(e, _S, m=m, wsb=wsb, lvs=lvs, psG=psG, psUp=psUp):
                e.wait_ge(S["sq"], lvs)
                for k in range(8):
                    e.matmul(psG[:, :], wsb[:, k, :], rq3[:, k, :],
                             start=(k == 0), stop=(k == 7))
                op = None
                for k in range(8):
                    op = e.matmul(psUp[:, :], wsb[:, 8 + k, :], rq3[:, k, :],
                                  start=(k == 0), stop=(k == 7))
                return op
            tm = tmark(t_m)
            ws_last[0] = tm
            def s_g(e, _S, tm=tm, psG=psG):
                e.wait_ge(S["stt"], tm)
                return e.activation(out=SGT[:, :], in_=psG[:, :], func=AF.Silu)
            sg = smark(s_g)
            def v_h(e, _S, sg=sg, psUp=psUp):
                e.wait_ge(S["ssc"], sg)
                return e.tensor_tensor(out=HM[:, :], in0=SGT[:, :],
                                       in1=psUp[:, :], op=OP.mult)
            vh = vmark(v_h)
            sch.wait("tensor", "sv", vh)
            pieces = [(HM[:, 128 * jj:128 * (jj + 1)], 128 * jj)
                      for jj in range(4)]
            tv = transpose_batch(pieces, psT1, IDB, wait_sv=vh)
            sch.wait("vector", "stt", tv)
            ev = vmark(lambda e, _S, m=m: e.tensor_copy(
                PB[:, pbHT:pbHT + 4 * NI].rearrange(
                    "p (j f) -> p j f", f=NI)[:, :, 128 * m:128 * (m + 1)],
                psT1[:, 0:512].rearrange("p (j f) -> p j f", f=128)))
            sch.wait("tensor", "sv", ev)
        for j in range(4 if parts & 2 else 0):
            quant_tile(PB[:, pbHT + NI * j:pbHT + NI * (j + 1)],
                       PB[:, pbHQ + NI * j:pbHQ + NI * (j + 1)], NI)
        for j in range(4 if parts & 4 else 0):
            for b0 in range(0, KNI, 8):
                nb8 = min(8, KNI - b0)
                pieces = [(PB[:, pbHQ + NI * j + 128 * (b0 + jj):
                              pbHQ + NI * j + 128 * (b0 + jj + 1)], 128 * jj)
                          for jj in range(nb8)]
                tv = transpose_batch(pieces, psT2, IDB,
                                     wait_sv=sch.cnt["sv"])
                sch.wait("vector", "stt", tv)
                ev = vmark(lambda e, _S, j=j, b0=b0, nb8=nb8: e.tensor_copy(
                    PB[:, pbRHQ:pbRHQ + KNI * 512].rearrange(
                        "p (k c) -> p k c", c=512)[:, b0:b0 + nb8,
                                                   128 * j:128 * (j + 1)],
                    psT2[:, 0:128 * nb8].rearrange(
                        "p (k c) -> p k c", c=128)))
                sch.wait("tensor", "sv", ev)
        rhq3 = PB[:, pbRHQ:pbRHQ + KNI * 512].rearrange("p (k c) -> p k c",
                                                        c=512)
        for mo in range(8 if parts & 4 else 0):
            wsb = WS
            sch.wait("sync", "stt", ws_last[0])
            lvs = 0
            for kt in range(KNI):
                lvs = ld(wsb[:, kt, :], d_wqd[kt][:, 128 * mo:128 * (mo + 1)])
            psDn = psC if mo % 2 == 0 else psA
            _svn = sch.cnt["sv"]
            def t_d(e, _S, mo=mo, wsb=wsb, lvs=lvs, w=_svn, psDn=psDn):
                e.wait_ge(S["sq"], lvs)
                e.wait_ge(S["sv"], w)
                op = None
                for kt in range(KNI):
                    op = e.matmul(psDn[:, :], wsb[:, kt, :], rhq3[:, kt, :],
                                  start=(kt == 0), stop=(kt == KNI - 1))
                return op
            td = tmark(t_d)
            ws_last[0] = td
            def v_e(e, _S, mo=mo, td=td, psDn=psDn):
                e.wait_ge(S["stt"], td)
                return e.tensor_copy(PB[:, pbFDB + 512 * mo:
                                        pbFDB + 512 * (mo + 1)], psDn[:, :])
            ve = vmark(v_e)
            sch.wait("tensor", "sv", ve)
        for j in range(4 if parts & 4 else 0):
            pieces = [(PB[:, pbFDB + 512 * mo + 128 * j:
                          pbFDB + 512 * mo + 128 * (j + 1)], 128 * mo)
                      for mo in range(8)]
            tv = transpose_batch(pieces, psT1, IDB, wait_sv=sch.cnt["sv"])
            out_fn(j, tv)

    for c in range(0 if (phases < 2 and phases != -1) else (1 if sub & 256 else NCH)):
        t0 = 512 * c
        sch.wait("sync", "sv", sch.cnt["sv"])
        sch.wait("sync", "sgd", sch.cnt["sgd"])
        lv = 0
        for j in range(4):
            lv = ld(PF[:, oX0 + 1024 * j:oX0 + 1024 * (j + 1)],
                    x0[t0 + 128 * j:t0 + 128 * (j + 1), :])
        if sub & 33554432:
            for mo in range(1):
                wsb = WS
                lvs2 = 0
                for k in range(8):
                    lvs2 = ld(wsb[:, k, :],
                              d_winT[k][:, 128 * mo:128 * (mo + 1)])
                def t_w0(e, _S, lvs2=lvs2):
                    e.wait_ge(S["sq"], lvs2)
                    op = None
                    for k in range(8):
                        op = e.matmul(psD[:, :], wsb[:, k, :],
                                      PB[:, bRQ:bRQ + 4096].rearrange(
                                          "p (k c) -> p k c", c=512)[:, k, :],
                                      start=(k == 0), stop=(k == 7))
                    return op
                tw0 = tmark(t_w0)
                def v_zb0(e, _S, tw0=tw0):
                    e.wait_ge(S["stt"], tw0)
                    return e.tensor_copy(ZB[:, :], psD[:, :])
                vmark(v_zb0)
        sch.wait("vector", "sq", lv)
        for j in range(0 if sub & 524288 else 4):
            _, su = ln_stats(PF[:, oX0 + 1024 * j:oX0 + 1024 * (j + 1)], j)
            ln_apply(PF[:, oX0 + 1024 * j:oX0 + 1024 * (j + 1)],
                     PF[:, oSC + 1024 * j:oSC + 1024 * (j + 1)], j, su)

        def x1_out(j, tv):
            def v1(e, _S, j=j, tv=tv):
                e.wait_ge(S["stt"], tv)
                return e.scalar_tensor_tensor(
                    out=PF[:, oX1 + 1024 * j:oX1 + 1024 * (j + 1)],
                    in0=psT1[:, 0:1024], scalar=0.5,
                    in1=PF[:, oX0 + 1024 * j:oX0 + 1024 * (j + 1)],
                    op0=OP.mult, op1=OP.add)
            ve = vmark(v1)
            sch.wait("tensor", "sv", ve)

        _parts = (1 if sub & 16 else 0) | (2 if sub & 32 else 0) | \
            (4 if sub & 64 else 0)
        emit_ffn(c, oSC, bXQ, bRQ, bHTOK, bHQ, bRHQ, bFDB,
                 -1 if sub & 1048576 else c, x1_out, parts=_parts)

        for j in range(4 if sub & 128 else 0):
            _, su = ln_stats(PF[:, oX1 + 1024 * j:oX1 + 1024 * (j + 1)], j)
            ln_apply(PF[:, oX1 + 1024 * j:oX1 + 1024 * (j + 1)],
                     PF[:, oYT + 1024 * j:oYT + 1024 * (j + 1)], j, su)
            vx = vmark(lambda e, _S, j=j: e.tensor_tensor(
                out=PF[:, oSC + 1024 * j:oSC + 1024 * (j + 1)],
                in0=PF[:, oX1 + 1024 * j:oX1 + 1024 * (j + 1)],
                in1=PF[:, oYT + 1024 * j:oYT + 1024 * (j + 1)], op=OP.add))
            sch.wait("gpsimd", "sv", vx)
            stg(d_x1y[t0 + 128 * j:t0 + 128 * (j + 1), :],
                PF[:, oSC + 1024 * j:oSC + 1024 * (j + 1)])
            _, su2 = ln_stats(PF[:, oYT + 1024 * j:oYT + 1024 * (j + 1)], j)
            sch.wait("vector", "sgd", sch.cnt["sgd"])
            ln_apply(PF[:, oYT + 1024 * j:oYT + 1024 * (j + 1)],
                     PF[:, oSC + 1024 * j:oSC + 1024 * (j + 1)], j, su2)

            def v_gb(e, _S, j=j):
                e.tensor_tensor(
                    out=PB[:, bXN2 + 1024 * j:bXN2 + 1024 * (j + 1)],
                    in0=PF[:, oSC + 1024 * j:oSC + 1024 * (j + 1)],
                    in1=GMBC[:, :], op=OP.mult)
                return e.tensor_tensor(
                    out=PB[:, bXN2 + 1024 * j:bXN2 + 1024 * (j + 1)],
                    in0=PB[:, bXN2 + 1024 * j:bXN2 + 1024 * (j + 1)],
                    in1=BEBC[:, :], op=OP.add)
            vmark(v_gb)
        for j in range(0 if sub & 262144 else (4 if sub & 128 else 0)):
            pieces = [(PB[:, bXN2 + 1024 * j + 128 * k:
                          bXN2 + 1024 * j + 128 * (k + 1)], 128 * k)
                      for k in range(8)]
            tv = transpose_batch(pieces, psT2, IDB, wait_sv=sch.cnt["sv"])
            sch.wait("vector", "stt", tv)
            ev = vmark(lambda e, _S, j=j: e.tensor_copy(
                PB[:, bRQ:bRQ + 4096].rearrange(
                    "p (k c) -> p k c", c=512)[:, :, 128 * j:128 * (j + 1)],
                psT2[:, :].rearrange("p (k c) -> p k c", c=128)))
            sch.wait("tensor", "sv", ev)
        rx3 = PB[:, bRQ:bRQ + 4096].rearrange("p (k c) -> p k c", c=512)
        bWI = bHTOK
        for mo in range(32 if sub & 512 else 0):
            wsb = WS
            sch.wait("sync", "stt", ws_last[0])
            lvs = 0
            for k in range(8):
                lvs = ld(wsb[:, k, :], d_winT[k][:, 128 * mo:128 * (mo + 1)])
            psDn = psC if mo % 2 == 0 else psA
            _svn = sch.cnt["sv"]

            def t_wi(e, _S, mo=mo, wsb=wsb, lvs=lvs, w=_svn, psDn=psDn):
                e.wait_ge(S["sq"], lvs)
                e.wait_ge(S["sv"], w)
                op = None
                for k in range(8):
                    op = e.matmul(psDn[:, :], wsb[:, k, :], rx3[:, k, :],
                                  start=(k == 0), stop=(k == 7))
                return op
            td = tmark(t_wi)
            ws_last[0] = td

            def v_wi(e, _S, mo=mo, td=td, psDn=psDn):
                e.wait_ge(S["stt"], td)
                return e.tensor_copy(PB[:, bWI + 512 * mo:
                                        bWI + 512 * (mo + 1)], psDn[:, :])
            ve = vmark(v_wi)
            sch.wait("tensor", "sv", ve)
        if sub & 512:
            for mo in range(32):
                sch.wait("gpsimd", "sv", sch.cnt["sv"])
                if mo < 16:
                    stg(d_xb[mo][:, t0:t0 + 512],
                        PB[:, bWI + 512 * mo:bWI + 512 * (mo + 1)])
                else:
                    stg(d_z[mo - 16][:, t0:t0 + 512],
                        PB[:, bWI + 512 * mo:bWI + 512 * (mo + 1)])
    if phases >= 0 and phases >= 2:
        sch.barrier(tag="ph2")

    pbanks = [psA, psB, psC, psD]
    for half in range(2 if phases >= 3 else 0):
        h0 = 2048 * half
        for dt in range(16):
            vz = vmark(lambda e, _S: e.memset(PB[:, bXBT:bXBT + 3], 0.0))
            if half == 0:
                lv = ld(PB[:, bXBT + 3:bXBT + 3 + 2048], d_xb[dt][:, 0:2048])
            else:
                lv = ld(PB[:, bXBT:bXBT + 3 + 2048],
                        d_xb[dt][:, 2048 - 3:4096])
            sch.wait("vector", "sq", lv)

            def v_cv(e, _S, dt=dt):
                e.tensor_scalar(out=PF[:, oCVA:oCVA + 2048],
                                in0=PB[:, bXBT:bXBT + 2048],
                                scalar1=CW[:, dt, 0:1], scalar2=CB[:, dt:dt + 1],
                                op0=OP.mult, op1=OP.add)
                e.scalar_tensor_tensor(out=PF[:, oCVB:oCVB + 2048],
                                       in0=PB[:, bXBT + 1:bXBT + 1 + 2048],
                                       scalar=CW[:, dt, 1:2],
                                       in1=PF[:, oCVA:oCVA + 2048],
                                       op0=OP.mult, op1=OP.add)
                e.scalar_tensor_tensor(out=PF[:, oCVA:oCVA + 2048],
                                       in0=PB[:, bXBT + 2:bXBT + 2 + 2048],
                                       scalar=CW[:, dt, 2:3],
                                       in1=PF[:, oCVB:oCVB + 2048],
                                       op0=OP.mult, op1=OP.add)
                return e.scalar_tensor_tensor(
                    out=PF[:, oCVB:oCVB + 2048],
                    in0=PB[:, bXBT + 3:bXBT + 3 + 2048],
                    scalar=CW[:, dt, 3:4], in1=PF[:, oCVA:oCVA + 2048],
                    op0=OP.mult, op1=OP.add)
            vc = vmark(v_cv)

            def s_si(e, _S, vc=vc):
                e.wait_ge(S["sv"], vc)
                return e.activation(out=PB[:, bXCT:bXCT + 2048],
                                    in_=PF[:, oCVB:oCVB + 2048], func=AF.Silu)
            ss_ = smark(s_si)

            def t_wx(e, _S, dt=dt, ss_=ss_):
                e.wait_ge(S["ssc"], ss_)
                op = None
                for ci in range(4):
                    op = e.matmul(pbanks[ci][0:33, :], WXT[:, dt, :],
                                  PB[:, bXCT + 512 * ci:bXCT + 512 * (ci + 1)],
                                  start=(dt == 0), stop=(dt == 15))
                return op
            td = tmark(t_wx)

            sch.wait("gpsimd", "ssc", ss_)
            stg(d_xc_t(dt)[:, h0:h0 + 2048], PB[:, bXCT:bXCT + 2048])
            sch.wait("scalar", "stt", td)
            sch.wait("sync", "ssc", sch.cnt["ssc"])
            sch.wait("sync", "sgd", sch.cnt["sgd"])
        sch.wait("vector", "stt", sch.cnt["stt"])
        qbflat = qB[:, :, :].rearrange("p a b -> p (a b)")
        sg2 = 0
        for ci in range(4):
            ch = 4 * half + ci
            eslot = PF[:, oSC + 512 * ci:oSC + 512 * (ci + 1)]
            bslot = qbflat[:, 512 * ci:512 * (ci + 1)]
            ve = vmark(lambda e, _S, ci=ci, eslot=eslot: e.tensor_copy(
                eslot[0:33, :], pbanks[ci][0:33, :]))
            se = smark(lambda e, _S, ve=ve, eslot=eslot, bslot=bslot: (
                e.wait_ge(S["sv"], ve),
                e.activation(out=bslot[0:33, :], in_=eslot[0:33, :],
                             func=AF.Copy))[-1])
            sch.wait("gpsimd", "sv", ve)
            sch.wait("gpsimd", "ssc", se)
            sg1 = stg(d_dtr[0:1, 512 * ch:512 * (ch + 1)], eslot[0:1, :])
            sg2 = stg(d_bc[:, 512 * ch:512 * (ch + 1)], bslot[1:33, :])
        sch.wait("tensor", "sgd", sg2)
        sch.wait("vector", "sgd", sg2)
        sch.wait("scalar", "sgd", sg2)
    sch.barrier(tag="ph3")

    ht3 = PB[:, sHT:sHT + 16 * SCH].rearrange("p (n c) -> p n c", c=SCH)
    pt3 = PB[:, sPT:sPT + 16 * SCH].rearrange("p (n c) -> p n c", c=SCH)
    bbc3 = PB[:, sBBC:sBBC + 16 * SCH].rearrange("p (n c) -> p n c", c=SCH)
    cbc3 = PB[:, sCBC:sCBC + 16 * SCH].rearrange("p (n c) -> p n c", c=SCH)
    tra3 = PF[:, oTRA:oTRA + 8 * 512].rearrange("p (n c) -> p n c", c=SCH)
    trb3 = qB[:, :, :].rearrange("p a b -> p (a b)")[:, 0:4 * SCH] \
        .rearrange("p (n c) -> p n c", c=SCH)
    for sc in range(NCH if phases >= 4 else 0):
        t0 = SCH * sc
        sch.wait("sync", "sv", sch.cnt["sv"])
        ld(PF[:, oDTR:oDTR + SCH].rearrange("p (o c) -> p o c", o=1),
           bcast_dram(d_dtr[0:1, t0:t0 + SCH]))
        ld(bbc3, bcast_dram(d_bc[0:16, t0:t0 + SCH]))
        lv = ld(cbc3, bcast_dram(d_bc[16:32, t0:t0 + SCH]))
        for dt in range(16):
            lx = ld(XCS[:, :], d_xc_t(dt)[:, t0:t0 + SCH])

            _ssc_now = sch.cnt["ssc"]
            def v_pre(e, _S, dt=dt, lv=lv, lx=lx, w=_ssc_now):
                e.wait_ge(S["sq"], max(lv, lx))
                e.wait_ge(S["ssc"], w)
                return e.tensor_scalar(out=PF[:, oPRE:oPRE + SCH],
                                       in0=PF[:, oDTR:oDTR + SCH],
                                       scalar1=WDT[:, dt:dt + 1],
                                       scalar2=BDT[:, dt:dt + 1],
                                       op0=OP.mult, op1=OP.add)
            vp = vmark(v_pre)

            def s_dt(e, _S, vp=vp):
                e.wait_ge(S["sv"], vp)
                e.activation(out=PF[:, oSPA:oSPA + SCH],
                             in_=PF[:, oPRE:oPRE + SCH], func=AF.Exp)
                return e.activation(out=DTTB[:, :],
                                    in_=PF[:, oSPA:oSPA + SCH], func=AF.Ln,
                                    bias=1.0)
            sd = smark(s_dt)

            def v_dtx(e, _S, sd=sd):
                e.wait_ge(S["ssc"], sd)
                return e.tensor_tensor(out=DTXB[:, :], in0=DTTB[:, :],
                                       in1=XCS[:, :], op=OP.mult)
            vmark(v_dtx)
            for n in range(NST):
                def s_ab(e, _S, dt=dt, n=n, sd=sd):
                    if n == 0:
                        e.wait_ge(S["ssc"], sd)
                    return e.activation(out=ABF[:, :], in_=DTTB[:, :],
                                        func=AF.Exp,
                                        scale=A_sb[:, dt, n:n + 1])
                sab = smark(s_ab)

                def v_scan(e, _S, dt=dt, n=n, sab=sab):
                    e.wait_ge(S["ssc"], sab)
                    e.tensor_tensor(out=UBF[:, :], in0=DTXB[:, :],
                                    in1=bbc3[:, n, :], op=OP.mult)
                    return e.tensor_tensor_scan(
                        out=ht3[:, n, :], data0=ABF[:, :], data1=UBF[:, :],
                        initial=HLAST[:, 16 * dt + n:16 * dt + n + 1],
                        op0=OP.mult, op1=OP.add)
                vs = vmark(v_scan)
                sch.wait("scalar", "sv", vs)
                if dbg == 13 and sc == 0 and dt == 0 and n == 0:
                    sch.wait("gpsimd", "sv", vs)
                    stg(d_dbgb[0][:, 0:512], DTTB[:, :])
                    stg(d_dbgb[0][:, 512:1024], ABF[:, :])
                    stg(d_dbgb[0][:, 1024:1536], UBF[:, :])
                    stg(d_dbgb[0][:, 1536:2048], ht3[:, 0, :])
                    stg(d_dbgb[1][:, 0:512], XCS[:, :])
                    stg(d_dbgb[1][:, 512:1024], DTXB[:, :])
                    stg(d_dbgb[1][:, 1024:1536], bbc3[:, 0, :])
                    sch.wait("vector", "sgd", sch.cnt["sgd"])
                    sch.wait("scalar", "sgd", sch.cnt["sgd"])

            def v_fin(e, _S, dt=dt):
                e.tensor_copy(HLAST[:, 16 * dt:16 * (dt + 1)],
                              ht3[:, :, SCH - 1:SCH].rearrange(
                                  "p n o -> p (n o)"))
                e.tensor_tensor(out=pt3.rearrange("p n c -> p (n c)"),
                                in0=ht3.rearrange("p n c -> p (n c)"),
                                in1=cbc3.rearrange("p n c -> p (n c)"),
                                op=OP.mult)
                e.tensor_tensor(out=tra3.rearrange("p n c -> p (n c)"),
                                in0=pt3[:, 0:8, :].rearrange(
                                    "p n c -> p (n c)"),
                                in1=pt3[:, 8:16, :].rearrange(
                                    "p n c -> p (n c)"),
                                op=OP.add)
                e.tensor_tensor(out=trb3.rearrange("p n c -> p (n c)"),
                                in0=tra3[:, 0:4, :].rearrange(
                                    "p n c -> p (n c)"),
                                in1=tra3[:, 4:8, :].rearrange(
                                    "p n c -> p (n c)"),
                                op=OP.add)
                e.tensor_tensor(out=tra3[:, 0:2, :].rearrange(
                                    "p n c -> p (n c)"),
                                in0=trb3[:, 0:2, :].rearrange(
                                    "p n c -> p (n c)"),
                                in1=trb3[:, 2:4, :].rearrange(
                                    "p n c -> p (n c)"),
                                op=OP.add)
                e.tensor_tensor(out=PF[:, oYD:oYD + SCH],
                                in0=tra3[:, 0, :], in1=tra3[:, 1, :],
                                op=OP.add)
                return e.scalar_tensor_tensor(
                    out=YPBT[:, :], in0=XCS[:, :], scalar=DP[:, dt:dt + 1],
                    in1=PF[:, oYD:oYD + SCH], op0=OP.mult, op1=OP.add)
            vf = vmark(v_fin)
            sch.wait("gpsimd", "sv", vf)
            sg3 = stg(d_yp[dt][:, t0:t0 + SCH], YPBT[:, :])
            sch.wait("vector", "sgd", sg3)
            sch.wait("sync", "sv", vf)
    sch.barrier(tag="ph4")

    zt3 = PB[:, tZT:tZT + 16 * 512].rearrange("p (n c) -> p n c", c=512)
    yp3 = PB[:, tYP:tYP + 16 * 512].rearrange("p (n c) -> p n c", c=512)
    gg3 = PB[:, tGG:tGG + 16 * 512].rearrange("p (n c) -> p n c", c=512)
    for c in range(NCH if phases >= 5 else 0):
        t0 = 512 * c
        sch.wait("sync", "sv", sch.cnt["sv"])
        sch.wait("sync", "ssc", sch.cnt["ssc"])
        lvz = 0
        for dt in range(16):
            ld(zt3[:, dt, :], d_z[dt][:, t0:t0 + 512])
            lvz = ld(yp3[:, dt, :], d_yp[dt][:, t0:t0 + 512])
        lvx = 0
        for j in range(4):
            lvx = ld(PF[:, oX0 + 1024 * j:oX0 + 1024 * (j + 1)],
                     d_x1y[t0 + 128 * j:t0 + 128 * (j + 1), :])
        def s_sz(e, _S, lvz=lvz):
            e.wait_ge(S["sq"], lvz)
            op = None
            for dt in range(16):
                op = e.activation(out=gg3[:, dt, :], in_=zt3[:, dt, :],
                                  func=AF.Silu)
            return op
        ssz = smark(s_sz)

        def v_g(e, _S, ssz=ssz):
            e.wait_ge(S["ssc"], ssz)
            return e.tensor_tensor(out=zt3.rearrange("p n c -> p (n c)"),
                                   in0=gg3.rearrange("p n c -> p (n c)"),
                                   in1=yp3.rearrange("p n c -> p (n c)"),
                                   op=OP.mult)
        vg = vmark(v_g)
        for mo in range(8):
            wsb = WS
            sch.wait("sync", "stt", ws_last[0])
            lvs = 0
            for k in range(16):
                lvs = ld(wsb[:, k, :], d_woutT[k][:, 128 * mo:128 * (mo + 1)])
            psO = psS if mo % 2 == 0 else psD
            def t_o(e, _S, mo=mo, wsb=wsb, lvs=lvs, vg=vg, psO=psO):
                e.wait_ge(S["sq"], lvs)
                e.wait_ge(S["sv"], vg)
                op = None
                for k in range(16):
                    op = e.matmul(psO[:, :], wsb[:, k, :], zt3[:, k, :],
                                  start=(k == 0), stop=(k == 15))
                return op
            to = tmark(t_o)
            ws_last[0] = to
            def v_eo(e, _S, mo=mo, to=to, psO=psO):
                e.wait_ge(S["stt"], to)
                return e.tensor_copy(PB[:, tMOB + 512 * mo:
                                        tMOB + 512 * (mo + 1)], psO[:, :])
            veo = vmark(v_eo)
            sch.wait("tensor", "sv", veo)
        for j in range(4):
            pieces = [(PB[:, tMOB + 512 * mo + 128 * j:
                          tMOB + 512 * mo + 128 * (j + 1)], 128 * mo)
                      for mo in range(8)]
            tv = transpose_batch(pieces, psT1, IDB, wait_sv=sch.cnt["sv"])

            def v_x2(e, _S, j=j, tv=tv, lvx=lvx):
                e.wait_ge(S["stt"], tv)
                e.wait_ge(S["sq"], lvx)
                return e.tensor_tensor(
                    out=PF[:, oX1 + 1024 * j:oX1 + 1024 * (j + 1)],
                    in0=psT1[:, 0:1024],
                    in1=PF[:, oX0 + 1024 * j:oX0 + 1024 * (j + 1)],
                    op=OP.add)
            vx2 = vmark(v_x2)
            sch.wait("tensor", "sv", vx2)
        for j in range(4):
            _, su = ln_stats(PF[:, oX1 + 1024 * j:oX1 + 1024 * (j + 1)], j)
            ln_apply(PF[:, oX1 + 1024 * j:oX1 + 1024 * (j + 1)],
                     PF[:, oYT + 1024 * j:oYT + 1024 * (j + 1)], j, su)

        def x3_out(j, tv, t0=t0):
            def v1(e, _S, j=j, tv=tv):
                e.wait_ge(S["stt"], tv)
                return e.scalar_tensor_tensor(
                    out=PF[:, oSC + 1024 * j:oSC + 1024 * (j + 1)],
                    in0=psT1[:, 0:1024], scalar=0.5,
                    in1=PF[:, oX1 + 1024 * j:oX1 + 1024 * (j + 1)],
                    op0=OP.mult, op1=OP.add)
            vo = vmark(v1)
            sch.wait("tensor", "sv", vo)
            sch.wait("gpsimd", "sv", vo)
            sch.mark("gpsimd", lambda e, _S, j=j, t0=t0:
                     e.dma_start(out=out[t0 + 128 * j:t0 + 128 * (j + 1), :],
                                 in_=PF[:, oSC + 1024 * j:
                                        oSC + 1024 * (j + 1)]),
                     "sod", 16)

        emit_ffn(c, oYT, t2XQ, t2RQ, t2HT, t2HQ, t2RHQ, t2FDB, c, x3_out)
    if phases < 5 or phases == -1:
        sch.wait("gpsimd", "sv", sch.cnt["sv"])
        sch.wait("gpsimd", "ssc", sch.cnt["ssc"])
        sch.wait("gpsimd", "stt", sch.cnt["stt"])
        sch.wait("gpsimd", "sq", sch.cnt["sq"])
        sch.mark("gpsimd", lambda e, _S:
                 e.dma_start(out=out[0:128, :], in_=PF[:, 0:1024]),
                 "sod", 16)
    if dbg:
        sch.wait("gpsimd", "sv", sch.cnt["sv"])
        sch.wait("gpsimd", "ssc", sch.cnt["ssc"])
        sch.wait("gpsimd", "stt", sch.cnt["stt"])
        sch.wait("gpsimd", "sgd", sch.cnt["sgd"])
        sch.wait("gpsimd", "sq", sch.cnt["sq"])
        ob = out.bitcast(BF16).rearrange("a b -> (a b)")
        if dbg == 1:
            sch.mark("gpsimd", lambda e, _S: e.dma_start(
                out=out, in_=d_x1y), "sod", 16)
        elif dbg == 11:
            sch.mark("gpsimd", lambda e, _S: e.dma_start(
                out=out[0:4, 0:1024].rearrange("a b -> () (a b)"),
                in_=d_dtr), "sod", 16)
        else:
            taps = {2: [d_xcA, d_xcB], 3: [d_yp], 4: [d_z], 5: [d_bc],
                    6: [d_wqgu], 7: [d_winT], 8: [d_wqd, d_winT],
                    9: [d_woutT], 10: [d_xb], 12: [d_dbgb],
                    13: [d_dbgb]}[dbg]
            pos = 0
            for t in taps:
                n = 1
                for s in t.tensor.shape:
                    n *= s
                flat = bass.AP(tensor=t.tensor, offset=0, ap=[[1, n]])
                def dcp(e, _S, dst=ob[pos:pos + n], srcf=flat):
                    return e.dma_start(out=dst, in_=srcf)
                sch.mark("gpsimd", dcp, "sod", 16)
                pos += n
    sch.wait("gpsimd", "sod", sch.cnt["sod"])
    sch.wait("gpsimd", "sgd", sch.cnt["sgd"])
    sch.wait("sync", "sq", sch.cnt["sq"])

    with stck, nc.Block() as block:

        @block.sync
        def _(e):
            for fn in sch.q["sync"]:
                fn(e, S)

        @block.tensor
        def _(e):
            for fn in sch.q["tensor"]:
                fn(e, S)

        @block.vector
        def _(e):
            for fn in sch.q["vector"]:
                fn(e, S)

        @block.scalar
        def _(e):
            for fn in sch.q["scalar"]:
                fn(e, S)

        @block.gpsimd
        def _(e):
            for fn in sch.q["gpsimd"]:
                fn(e, S)

    return nc


_NC_CACHE = None


def kernel(L0_attn_x, L0_attn_ln_gamma, L0_attn_ln_beta, L0_attn_W_in,
           L0_attn_conv_w, L0_attn_conv_b, L0_attn_W_x, L0_attn_w_dt,
           L0_attn_b_dt, L0_attn_A_log, L0_attn_D_param, L0_attn_W_out,
           L0_ffn_w_gate, L0_ffn_w_up, L0_ffn_w_down):
    global _NC_CACHE
    if _NC_CACHE is None:
        _NC_CACHE = build_program(phases=5, sub=1791)
    nc = _NC_CACHE

    f32 = np.float32
    cw = np.ascontiguousarray
    base = {
        "ln_g": cw(np.asarray(L0_attn_ln_gamma, f32).reshape(1, D)),
        "ln_b": cw(np.asarray(L0_attn_ln_beta, f32).reshape(1, D)),
        "w_inT": cw(np.asarray(L0_attn_W_in, f32).T),
        "conv_w": cw(np.asarray(L0_attn_conv_w, f32)),
        "conv_b": cw(np.asarray(L0_attn_conv_b, f32).reshape(1, DI)),
        "w_xT": cw(np.asarray(L0_attn_W_x, f32).T),
        "w_dt": cw(np.asarray(L0_attn_w_dt, f32).reshape(1, DI)),
        "b_dt": cw(np.asarray(L0_attn_b_dt, f32).reshape(1, DI)),
        "A_log": cw(np.asarray(L0_attn_A_log, f32)),
        "D_p": cw(np.asarray(L0_attn_D_param, f32).reshape(1, DI)),
        "w_outT": cw(np.asarray(L0_attn_W_out, f32).T),
        "wg": cw(np.asarray(L0_ffn_w_gate, f32)),
        "wu": cw(np.asarray(L0_ffn_w_up, f32)),
        "wd": cw(np.asarray(L0_ffn_w_down, f32)),
        "id16": np.eye(128, dtype=ml_dtypes.bfloat16),
        "id32": np.eye(128, dtype=np.float32),
    }
    x = np.asarray(L0_attn_x, f32)
    in_maps = []
    for c in range(8):
        b = c // 2
        m = dict(base)
        m["x0"] = cw(x[b])
        in_maps.append(m)

    res = run_bass_kernel_spmd(nc, in_maps, core_ids=list(range(8)))
    outp = np.empty((4, T, D), np.float32)
    for b in range(4):
        outp[b, 0:2048] = res.results[2 * b]["out"][0:2048]
        outp[b, 2048:4096] = res.results[2 * b + 1]["out"][2048:4096]
    return outp
